# version 03
# baseline (speedup 1.0000x reference)
import numpy as np
from contextlib import ExitStack
import concourse.bass as bass
import concourse.mybir as mybir
from concourse.bass_utils import run_bass_kernel_spmd

F32 = mybir.dt.float32
BF16 = mybir.dt.bfloat16
AF = mybir.ActivationFunctionType
ALU = mybir.AluOpType
AX = mybir.AxisListType

NCORES = 8
B = 4
L = 8192
D = 1024
TLOC = 4096
DHID = 4096
GAMMA = 0.96875
EPS = 1e-5
HALO = 2048

_NC_CACHE = {}


def _cached(key, fn):
    if key not in _NC_CACHE:
        _NC_CACHE[key] = fn()
    return _NC_CACHE[key]


def _wpanel_src(w, o, I):
    return w[:, o * 128:(o + 1) * 128].rearrange("(c p) f -> p c f", p=128)


def _chunk3(t, width):
    return t[:].rearrange("p (c f) -> p c f", f=width)



def _ctx(nc, es):
    def sb(name, shape, dtype):
        return es.enter_context(nc.sbuf_tensor(name, shape, dtype))

    def psum(name, shape):
        return es.enter_context(nc.psum_tensor(name, shape, F32))

    def sem(name):
        return es.enter_context(nc.semaphore(name))
    return sb, psum, sem


def build_ln():
    nc = bass.Bass("TRN2", target_bir_lowering=False, debug=False)
    x = nc.dram_tensor("x", [TLOC, D], F32, kind="ExternalInput").ap()
    xn = nc.dram_tensor("xn", [TLOC, D], F32, kind="ExternalOutput").ap()
    NT = TLOC // 128
    with ExitStack() as es:
        sb, psum, sem = _ctx(nc, es)
        xall = sb("xall", [128, NT * D], F32)
        sq = sb("sq", [128, 2 * D], F32)
        ssum = sb("ssum", [128, NT], F32)
        ssq = sb("ssq", [128, NT], F32)
        stt_ = sb("stt", [128, 4 * NT], F32)
        ld = sem("ld")
        sc = sem("sc")
        vv = sem("vv")
        so = sem("so")
        block = es.enter_context(nc.Block())

        @block.sync
        def _(sync):
            sync.dma_start(out=_chunk3(xall, D),
                           in_=x[:, :].rearrange("(c p) f -> p c f", p=128)).then_inc(ld, 16)

        @block.scalar
        def _(s):
            for i in range(NT):
                s.wait_ge(ld, 16)
                if i >= 2:
                    s.wait_ge(vv, i - 1)
                s.square(sq[:, (i % 2) * D:(i % 2 + 1) * D],
                         xall[:, i * D:(i + 1) * D]).then_inc(sc, 1)
            s.wait_ge(vv, NT + 4)
            s.activation(out=stt_[:, 2 * NT:3 * NT], in_=stt_[:, NT:2 * NT],
                         func=AF.Sqrt).then_inc(sc, 1)

        @block.vector
        def _(v):
            v.wait_ge(ld, 16)
            for i in range(NT):
                v.reduce_sum(ssum[:, i:i + 1], xall[:, i * D:(i + 1) * D], axis=AX.X)
                v.wait_ge(sc, i + 1)
                inst = v.reduce_sum(ssq[:, i:i + 1], sq[:, (i % 2) * D:(i % 2 + 1) * D],
                                    axis=AX.X)
                inst.then_inc(vv, 1)
            v.wait_ge(vv, NT)
            v.tensor_scalar_mul(stt_[:, 0:NT], ssum[:, :], 1.0 / D).then_inc(vv, 1)
            v.wait_ge(vv, NT + 1)
            v.tensor_mul(stt_[:, NT:2 * NT], stt_[:, 0:NT], stt_[:, 0:NT]).then_inc(vv, 1)
            v.wait_ge(vv, NT + 2)
            v.scalar_tensor_tensor(stt_[:, NT:2 * NT], ssq[:, :], 1.0 / D,
                                   stt_[:, NT:2 * NT], ALU.mult, ALU.subtract).then_inc(vv, 1)
            v.wait_ge(vv, NT + 3)
            v.tensor_scalar_add(stt_[:, NT:2 * NT], stt_[:, NT:2 * NT], EPS).then_inc(vv, 1)
            v.wait_ge(sc, NT + 1)
            v.reciprocal(stt_[:, 3 * NT:4 * NT], stt_[:, 2 * NT:3 * NT]).then_inc(vv, 1)
            v.wait_ge(vv, NT + 5)
            for i in range(NT):
                v.tensor_scalar(xall[:, i * D:(i + 1) * D], xall[:, i * D:(i + 1) * D],
                                stt_[:, i:i + 1], stt_[:, 3 * NT + i:3 * NT + i + 1],
                                ALU.subtract, ALU.mult).then_inc(vv, 1)

        @block.gpsimd
        def _(g):
            for i in range(NT):
                g.wait_ge(vv, NT + 5 + i + 1)
                g.dma_start(out=xn[i * 128:(i + 1) * 128, :],
                            in_=xall[:, i * D:(i + 1) * D]).then_inc(so, 16)
            g.wait_ge(so, 16 * NT)
    return nc


def build_ffn_a():
    nc = bass.Bass("TRN2", target_bir_lowering=False, debug=False)
    xnT = nc.dram_tensor("xnT", [D, TLOC], F32, kind="ExternalInput").ap()
    w1t = nc.dram_tensor("w1t", [D, DHID], F32, kind="ExternalInput").ap()
    w2t = nc.dram_tensor("w2t", [D, DHID], F32, kind="ExternalInput").ap()
    s1 = nc.dram_tensor("s1", [128, DHID // 128], F32, kind="ExternalInput").ap()
    s2 = nc.dram_tensor("s2", [128, DHID // 128], F32, kind="ExternalInput").ap()
    h1 = nc.dram_tensor("h1", [DHID, TLOC], BF16, kind="ExternalOutput").ap()
    I = D
    NK = I // 128
    NO = DHID // 128
    NTC = TLOC // 512
    NG = NO * NTC
    NB = 3
    with ExitStack() as es:
        sb, psum, sem = _ctx(nc, es)
        xf = sb("xf", [128, NK * 512], F32)
        xb = sb("xb", [128, NK * TLOC], BF16)
        wf1 = sb("wf1", [128, NB * I], F32)
        wf2 = sb("wf2", [128, NB * I], F32)
        wb1 = sb("wb1", [128, NB * I], BF16)
        wb2 = sb("wb2", [128, NB * I], BF16)
        sct = sb("sct", [128, 2 * (DHID // 128)], F32)
        t1 = sb("t1", [128, 2 * 512], F32)
        ob = sb("ob", [128, 2 * 512], BF16)
        pA0 = psum("pA0", [128, 512])
        pB0 = psum("pB0", [128, 512])
        pA1 = psum("pA1", [128, 512])
        pB1 = psum("pB1", [128, 512])
        ld = sem("ld")
        w1s = sem("w1s")
        w2s = sem("w2s")
        cvx = sem("cvx")
        cvw = sem("cvw")
        mm = sem("mm")
        scv = sem("scv")
        ev = sem("ev")
        so = sem("so")
        block = es.enter_context(nc.Block())
        PA = [pA0, pA1]
        PB = [pB0, pB1]

        def wslice(w, j):
            return w[:, (j % NB) * I:(j % NB + 1) * I]

        @block.sync
        def _(sync):
            sync.dma_start(out=sct[:, 0:NO], in_=s1[:, :]).then_inc(ld, 16)
            sync.dma_start(out=sct[:, NO:2 * NO], in_=s2[:, :]).then_inc(ld, 16)
            for tc in range(NTC):
                if tc >= 1:
                    sync.wait_ge(cvx, tc)
                sync.dma_start(
                    out=_chunk3(xf, 512),
                    in_=xnT[:, tc * 512:(tc + 1) * 512].rearrange("(c p) f -> p c f", p=128),
                ).then_inc(ld, 16)

        @block.gpsimd
        def _(g):
            def wdma1(o):
                if o >= NB:
                    g.wait_ge(cvw, o - NB + 1)
                g.dma_start(out=_chunk3(wslice(wf1, o), 128),
                            in_=_wpanel_src(w1t, o, I)).then_inc(w1s, 16)
            for o in range(min(NB, NO)):
                wdma1(o)
            g_idx = 0
            for o in range(NO):
                for tc in range(NTC):
                    par = g_idx % 2
                    g.wait_ge(ev, g_idx + 1)
                    g.dma_start(out=h1[o * 128:(o + 1) * 128, tc * 512:(tc + 1) * 512],
                                in_=ob[:, par * 512:(par + 1) * 512]).then_inc(so, 16)
                    g_idx += 1
                if o + NB < NO:
                    wdma1(o + NB)
            g.wait_ge(so, 16 * NG)

        @block.scalar
        def _(s):
            def wdma2(o):
                if o >= NB:
                    s.wait_ge(cvw, o - NB + 1)
                s.dma_start(out=_chunk3(wslice(wf2, o), 128),
                            in_=_wpanel_src(w2t, o, I)).then_inc(w2s, 16)
            for o in range(min(NB, NO)):
                wdma2(o)
            g_idx = 0
            for o in range(NO):
                for tc in range(NTC):
                    par = g_idx % 2
                    s.wait_ge(mm, 2 * g_idx + 1)
                    if g_idx >= 2:
                        s.wait_ge(ev, g_idx - 1)
                    s.activation(out=t1[:, par * 512:(par + 1) * 512], in_=PA[par][:],
                                 func=AF.Silu, scale=sct[:, o:o + 1]).then_inc(scv, 1)
                    g_idx += 1
                if o + NB < NO:
                    wdma2(o + NB)

        @block.vector
        def _(v):
            for tc in range(NTC):
                v.wait_ge(ld, 32 + 16 * (tc + 1))
                v.tensor_copy(
                    _chunk3(xb, TLOC)[:, :, tc * 512:(tc + 1) * 512],
                    _chunk3(xf, 512),
                ).then_inc(cvx, 1)

            def castw(o):
                v.wait_ge(w1s, 16 * (o + 1))
                v.wait_ge(w2s, 16 * (o + 1))
                v.tensor_copy(wslice(wb1, o), wslice(wf1, o))
                v.tensor_copy(wslice(wb2, o), wslice(wf2, o)).then_inc(cvw, 1)
            for o in range(min(NB, NO)):
                castw(o)
            g_idx = 0
            for o in range(NO):
                for tc in range(NTC):
                    par = g_idx % 2
                    v.wait_ge(mm, 2 * g_idx + 2)
                    v.wait_ge(scv, g_idx + 1)
                    if g_idx >= 2:
                        v.wait_ge(so, 16 * (g_idx - 1))
                    v.scalar_tensor_tensor(ob[:, par * 512:(par + 1) * 512],
                                           PB[par][:], sct[:, NO + o:NO + o + 1],
                                           t1[:, par * 512:(par + 1) * 512],
                                           ALU.mult, ALU.mult).then_inc(ev, 1)
                    g_idx += 1
                if o + NB < NO:
                    castw(o + NB)

        @block.tensor
        def _(t):
            t.wait_ge(cvx, NTC)
            g_idx = 0
            for o in range(NO):
                t.wait_ge(cvw, o + 1)
                for tc in range(NTC):
                    par = g_idx % 2
                    if g_idx >= 2:
                        t.wait_ge(ev, g_idx - 1)
                    for k in range(NK):
                        inst = t.matmul(PA[par][:],
                                        wslice(wb1, o)[:, k * 128:(k + 1) * 128],
                                        _chunk3(xb, TLOC)[:, k, tc * 512:(tc + 1) * 512],
                                        start=(k == 0), stop=(k == NK - 1))
                    inst.then_inc(mm, 1)
                    for k in range(NK):
                        inst = t.matmul(PB[par][:],
                                        wslice(wb2, o)[:, k * 128:(k + 1) * 128],
                                        _chunk3(xb, TLOC)[:, k, tc * 512:(tc + 1) * 512],
                                        start=(k == 0), stop=(k == NK - 1))
                    inst.then_inc(mm, 1)
                    g_idx += 1
    return nc


def build_ffn_mid():
    nc = bass.Bass("TRN2", target_bir_lowering=False, debug=False)
    hin = nc.dram_tensor("hin", [DHID, TLOC], BF16, kind="ExternalInput").ap()
    wt = nc.dram_tensor("wt", [DHID, DHID], F32, kind="ExternalInput").ap()
    sc_in = nc.dram_tensor("sc", [128, DHID // 128], F32, kind="ExternalInput").ap()
    hout = nc.dram_tensor("hout", [DHID, TLOC], BF16, kind="ExternalOutput").ap()
    I = DHID
    NK = I // 128
    NO = DHID // 128
    TH = TLOC // 2
    NTC = TH // 512
    NJ = 2 * NO
    with ExitStack() as es:
        sb, psum, sem = _ctx(nc, es)
        xb = sb("xb", [128, NK * TH], BF16)
        wf = sb("wf", [128, 2 * I], F32)
        wb = sb("wb", [128, 2 * I], BF16)
        sct = sb("sct", [128, DHID // 128], F32)
        ob = sb("ob", [128, 2 * 512], BF16)
        p0 = psum("p0", [128, 512])
        p1 = psum("p1", [128, 512])
        ld = sem("ld")
        wps = sem("wps")
        cvw = sem("cvw")
        mm = sem("mm")
        ev = sem("ev")
        so = sem("so")
        block = es.enter_context(nc.Block())
        PS = [p0, p1]

        @block.sync
        def _(sync):
            sync.dma_start(out=sct[:, :], in_=sc_in[:, :]).then_inc(ld, 16)
            for half in range(2):
                if half == 1:
                    sync.wait_ge(mm, NO * NTC)
                sync.dma_start(
                    out=_chunk3(xb, TH),
                    in_=hin[:, half * TH:(half + 1) * TH].rearrange("(c p) f -> p c f", p=128),
                ).then_inc(ld, 16)

        @block.gpsimd
        def _(g):
            def wdma(j):
                if j >= 2:
                    g.wait_ge(cvw, j - 1)
                g.dma_start(out=_chunk3(wf[:, (j % 2) * I:(j % 2 + 1) * I], 128),
                            in_=_wpanel_src(wt, j % NO, I)).then_inc(wps, 16)
            wdma(0)
            wdma(1)
            g_idx = 0
            for j in range(NJ):
                for tc in range(NTC):
                    par = g_idx % 2
                    g.wait_ge(ev, g_idx + 1)
                    o = j % NO
                    half = j // NO
                    g.dma_start(
                        out=hout[o * 128:(o + 1) * 128,
                                 half * TH + tc * 512:half * TH + (tc + 1) * 512],
                        in_=ob[:, par * 512:(par + 1) * 512]).then_inc(so, 16)
                    g_idx += 1
                if j + 2 < NJ:
                    wdma(j + 2)
            g.wait_ge(so, 16 * NJ * NTC)

        @block.vector
        def _(v):
            for j in range(NJ):
                v.wait_ge(wps, 16 * (j + 1))
                if j >= 2:
                    v.wait_ge(mm, NTC * (j - 1))
                v.tensor_copy(wb[:, (j % 2) * I:(j % 2 + 1) * I],
                              wf[:, (j % 2) * I:(j % 2 + 1) * I]).then_inc(cvw, 1)

        @block.tensor
        def _(t):
            g_idx = 0
            for half in range(2):
                t.wait_ge(ld, 16 * (half + 2))
                for o in range(NO):
                    j = half * NO + o
                    t.wait_ge(cvw, j + 1)
                    for tc in range(NTC):
                        par = g_idx % 2
                        if g_idx >= 2:
                            t.wait_ge(ev, g_idx - 1)
                        for k in range(NK):
                            inst = t.matmul(PS[par][:],
                                            wb[:, (j % 2) * I + k * 128:(j % 2) * I + (k + 1) * 128],
                                            _chunk3(xb, TH)[:, k, tc * 512:(tc + 1) * 512],
                                            start=(k == 0), stop=(k == NK - 1))
                        inst.then_inc(mm, 1)
                        g_idx += 1

        @block.scalar
        def _(s):
            g_idx = 0
            for j in range(NJ):
                o = j % NO
                for tc in range(NTC):
                    par = g_idx % 2
                    s.wait_ge(mm, g_idx + 1)
                    if g_idx >= 2:
                        s.wait_ge(so, 16 * (g_idx - 1))
                    s.activation(out=ob[:, par * 512:(par + 1) * 512], in_=PS[par][:],
                                 func=AF.Gelu, scale=sct[:, o:o + 1]).then_inc(ev, 1)
                    g_idx += 1
    return nc


def build_ffn_final():
    nc = bass.Bass("TRN2", target_bir_lowering=False, debug=False)
    hin = nc.dram_tensor("hin", [DHID, TLOC], BF16, kind="ExternalInput").ap()
    w3t = nc.dram_tensor("w3t", [DHID, D], F32, kind="ExternalInput").ap()
    s3 = nc.dram_tensor("s3", [128, D // 128], F32, kind="ExternalInput").ap()
    sT = nc.dram_tensor("sT", [D, TLOC], F32, kind="ExternalInput").ap()
    xT = nc.dram_tensor("xT", [D, TLOC], F32, kind="ExternalInput").ap()
    outT = nc.dram_tensor("outT", [D, TLOC], F32, kind="ExternalOutput").ap()
    I = DHID
    NK = I // 128
    NO = D // 128
    TH = TLOC // 2
    NTC = TH // 512
    NJ = 2 * NO
    NG = NJ * NTC
    with ExitStack() as es:
        sb, psum, sem = _ctx(nc, es)
        xb = sb("xb", [128, NK * TH], BF16)
        wf = sb("wf", [128, 2 * I], F32)
        wb = sb("wb", [128, 2 * I], BF16)
        sct = sb("sct", [128, D // 128], F32)
        sg = sb("sg", [128, 2 * 512], F32)
        srt = sb("srt", [128, 2 * 512], F32)
        xrt = sb("xrt", [128, 2 * 512], F32)
        ot = sb("ot", [128, 2 * 512], F32)
        p0 = psum("p0", [128, 512])
        p1 = psum("p1", [128, 512])
        ld = sem("ld")
        lsx = sem("lsx")
        wps = sem("wps")
        cvw = sem("cvw")
        mm = sem("mm")
        ev = sem("ev")
        vc = sem("vc")
        so = sem("so")
        block = es.enter_context(nc.Block())
        PS = [p0, p1]

        def group_pos(g_idx):
            half = g_idx // (NO * NTC)
            o = (g_idx // NTC) % NO
            tc = g_idx % NTC
            return half, o, tc

        @block.sync
        def _(sync):
            sync.dma_start(out=sct[:, :], in_=s3[:, :]).then_inc(ld, 16)
            for half in range(2):
                if half == 1:
                    sync.wait_ge(mm, NO * NTC)
                sync.dma_start(
                    out=_chunk3(xb, TH),
                    in_=hin[:, half * TH:(half + 1) * TH].rearrange("(c p) f -> p c f", p=128),
                ).then_inc(ld, 16)

        @block.gpsimd
        def _(g):
            def wdma(j):
                if j >= 2:
                    g.wait_ge(cvw, j - 1)
                g.dma_start(out=_chunk3(wf[:, (j % 2) * I:(j % 2 + 1) * I], 128),
                            in_=_wpanel_src(w3t, j % NO, I)).then_inc(wps, 16)
            wdma(0)
            wdma(1)
            g_idx = 0
            for j in range(NJ):
                for tc in range(NTC):
                    par = g_idx % 2
                    g.wait_ge(vc, g_idx + 1)
                    half, o, _tc = group_pos(g_idx)
                    t0 = half * TH + tc * 512
                    g.dma_start(out=outT[o * 128:(o + 1) * 128, t0:t0 + 512],
                                in_=ot[:, par * 512:(par + 1) * 512]).then_inc(so, 16)
                    g_idx += 1
                if j + 2 < NJ:
                    wdma(j + 2)
            g.wait_ge(so, 16 * NG)

        @block.vector
        def _(v):
            def castw(j):
                v.wait_ge(wps, 16 * (j + 1))
                if j >= 2:
                    v.wait_ge(mm, NTC * (j - 1))
                v.tensor_copy(wb[:, (j % 2) * I:(j % 2 + 1) * I],
                              wf[:, (j % 2) * I:(j % 2 + 1) * I]).then_inc(cvw, 1)
            castw(0)
            castw(1)
            g_idx = 0
            for j in range(NJ):
                for tc in range(NTC):
                    par = g_idx % 2
                    v.wait_ge(ev, g_idx + 1)
                    v.wait_ge(lsx, 32 * (g_idx + 1))
                    if g_idx >= 2:
                        v.wait_ge(so, 16 * (g_idx - 1))
                    v.tensor_mul(ot[:, par * 512:(par + 1) * 512],
                                 sg[:, par * 512:(par + 1) * 512],
                                 srt[:, par * 512:(par + 1) * 512])
                    v.tensor_add(ot[:, par * 512:(par + 1) * 512],
                                 ot[:, par * 512:(par + 1) * 512],
                                 xrt[:, par * 512:(par + 1) * 512]).then_inc(vc, 1)
                    g_idx += 1
                if j + 2 < NJ:
                    castw(j + 2)

        @block.tensor
        def _(t):
            g_idx = 0
            for half in range(2):
                t.wait_ge(ld, 16 * (half + 2))
                for o in range(NO):
                    j = half * NO + o
                    t.wait_ge(cvw, j + 1)
                    for tc in range(NTC):
                        par = g_idx % 2
                        if g_idx >= 2:
                            t.wait_ge(ev, g_idx - 1)
                        for k in range(NK):
                            inst = t.matmul(PS[par][:],
                                            wb[:, (j % 2) * I + k * 128:(j % 2) * I + (k + 1) * 128],
                                            _chunk3(xb, TH)[:, k, tc * 512:(tc + 1) * 512],
                                            start=(k == 0), stop=(k == NK - 1))
                        inst.then_inc(mm, 1)
                        g_idx += 1

        @block.scalar
        def _(s):
            def loads(g_idx):
                par = g_idx % 2
                if g_idx >= 2:
                    s.wait_ge(vc, g_idx - 1)
                half, o, tc = group_pos(g_idx)
                t0 = half * TH + tc * 512
                s.dma_start(out=srt[:, par * 512:(par + 1) * 512],
                            in_=sT[o * 128:(o + 1) * 128, t0:t0 + 512]).then_inc(lsx, 16)
                s.dma_start(out=xrt[:, par * 512:(par + 1) * 512],
                            in_=xT[o * 128:(o + 1) * 128, t0:t0 + 512]).then_inc(lsx, 16)
            loads(0)
            loads(1)
            for g_idx in range(NG):
                par = g_idx % 2
                s.wait_ge(mm, g_idx + 1)
                if g_idx >= 2:
                    s.wait_ge(vc, g_idx - 1)
                half, o, tc = group_pos(g_idx)
                s.activation(out=sg[:, par * 512:(par + 1) * 512], in_=PS[par][:],
                             func=AF.Silu, scale=sct[:, o:o + 1]).then_inc(ev, 1)
                if g_idx + 2 < NG:
                    loads(g_idx + 2)
    return nc

HHALO = 1536
NKT = 16


def build_retention():
    nc = bass.Bass("TRN2", target_bir_lowering=False, debug=False)
    TT_ = HHALO + TLOC
    NGRP = TT_ // 512
    NHG = HHALO // 512
    xnT = nc.dram_tensor("xnT", [D, TT_], F32, kind="ExternalInput").ap()
    wq = nc.dram_tensor("wq", [D, D], F32, kind="ExternalInput").ap()
    wk = nc.dram_tensor("wk", [D, D], F32, kind="ExternalInput").ap()
    wv = nc.dram_tensor("wv", [D, D], F32, kind="ExternalInput").ap()
    sT = nc.dram_tensor("sT", [D, TLOC], F32, kind="ExternalOutput").ap()
    NC_ = D // 128
    with ExitStack() as es:
        sb, psum, sem = _ctx(nc, es)
        xf = sb("xf", [128, 2 * NC_ * 512], F32)
        xb = sb("xb", [128, 2 * NC_ * 512], BF16)
        wqb = sb("wqb", [128, NC_ * D], BF16)
        wkb = sb("wkb", [128, NC_ * D], BF16)
        wvb = sb("wvb", [128, NC_ * D], BF16)
        kfr = sb("kfr", [128, NC_ * 4 * 512], BF16)
        vtr = sb("vtr", [128, NKT * D], BF16)
        qfb = sb("qfb", [128, NC_ * 512], BF16)
        scm = sb("scm", [128, NKT * 512], BF16)
        mrel = sb("mrel", [128, NKT * 512], BF16)
        gct = sb("gct", [128, 2], F32)
        yo = sb("yo", [128, 2 * 512], F32)
        et = yo[:, 0:512]
        ex = yo[:, 512:1024]
        ppA = psum("ppA", [128, 512])
        ppB = psum("ppB", [128, 512])
        psA = psum("psA", [128, 512])
        psB = psum("psB", [128, 512])
        pyA = psum("pyA", [128, 512])
        pyB = psum("pyB", [128, 512])
        ld = sem("ld")
        wls = sem("wls")
        wcv = sem("wcv")
        cvx = sem("cvx")
        mio = sem("mio")
        mex = sem("mex")
        mmk = sem("mmk")
        mmp = sem("mmp")
        evpv = sem("evpv")
        evps = sem("evps")
        mms = sem("mms")
        evs = sem("evs")
        mmy = sem("mmy")
        evy = sem("evy")
        so = sem("so")
        block = es.enter_context(nc.Block())

        PP = [ppA, ppB]
        PSC = [psA, psB]
        PY = [pyA, pyB]
        WB = {"q": wqb, "k": wkb, "v": wvb}
        WD = {"q": wq, "k": wk, "v": wv}

        OWN = list(range(NHG, NGRP))
        proj_engine = []
        PROJ_BASE = []
        for g in range(NGRP):
            PROJ_BASE.append(len(proj_engine))
            proj_engine += ["v"] * NC_
            if g >= NHG:
                proj_engine += ["v"] * NC_
            proj_engine += ["s"] * NC_
        NPROJ = len(proj_engine)
        cum_v = [0] * (NPROJ + 1)
        cum_s = [0] * (NPROJ + 1)
        for j in range(NPROJ):
            cum_v[j + 1] = cum_v[j] + (proj_engine[j] == "v")
            cum_s[j + 1] = cum_s[j] + (proj_engine[j] == "s")

        def wait_proj_free(eng, j):
            if j >= 2:
                prev = j - 2
                if proj_engine[prev] == "v":
                    eng.wait_ge(evpv, cum_v[prev + 1])
                else:
                    eng.wait_ge(evps, cum_s[prev + 1])

        NSC_BASE = {g: (g - NHG) * NKT for g in OWN}
        NY_BASE = {g: (g - NHG) * NC_ for g in OWN}

        @block.sync
        def _(sync):
            for idx, (nmw, half) in enumerate([(n, h) for n in "qkv" for h in range(2)]):
                if idx >= 1:
                    sync.wait_ge(wcv, idx)
                sync.dma_start(
                    out=_chunk3(xf[:, 0:NC_ * 512], 512),
                    in_=WD[nmw][:, half * 512:(half + 1) * 512]
                        .rearrange("(c p) f -> p c f", p=128),
                ).then_inc(wls, 16)
            for g in range(NGRP):
                if g == 0:
                    sync.wait_ge(wcv, 6)
                if g >= 2:
                    sync.wait_ge(cvx, g - 1)
                sync.dma_start(
                    out=_chunk3(xf[:, (g % 2) * NC_ * 512:((g % 2) + 1) * NC_ * 512], 512),
                    in_=xnT[:, g * 512:(g + 1) * 512].rearrange("(c p) f -> p c f", p=128),
                ).then_inc(ld, 16)

        @block.gpsimd
        def _(g_):
            for rel in range(NKT):
                g_.iota(et, [[1, 512]],
                        base=128 * (12 - rel), channel_multiplier=-1,
                        allow_small_or_imprecise_dtypes=True).then_inc(mio, 1)
                g_.wait_ge(mex, rel + 1)
                g_.affine_select(mrel[:, rel * 512:(rel + 1) * 512], ex,
                                 [[1, 512]], ALU.is_ge, 0.0,
                                 base=128 * (12 - rel),
                                 channel_multiplier=-1).then_inc(mmk, 1)
            n_y = 0
            for g in OWN:
                for o in range(NC_):
                    g_.wait_ge(evy, n_y + 1)
                    t0 = (g - NHG) * 512
                    g_.dma_start(out=sT[o * 128:(o + 1) * 128, t0:t0 + 512],
                                 in_=yo[:, (n_y % 2) * 512:(n_y % 2 + 1) * 512]
                                 ).then_inc(so, 16)
                    n_y += 1
            g_.wait_ge(so, 16 * len(OWN) * NC_)

        @block.vector
        def _(v):
            v.memset(gct[:, 0:1], GAMMA)
            for idx, (nmw, half) in enumerate([(n, h) for n in "qkv" for h in range(2)]):
                v.wait_ge(wls, 16 * (idx + 1))
                v.tensor_copy(
                    _chunk3(WB[nmw], D)[:, :, half * 512:(half + 1) * 512],
                    _chunk3(xf[:, 0:NC_ * 512], 512),
                ).then_inc(wcv, 1)
            for g in range(NGRP):
                v.wait_ge(ld, 16 * (g + 1))
                if g >= 2:
                    v.wait_ge(mmp, PROJ_BASE[g - 1])
                v.tensor_copy(xb[:, (g % 2) * NC_ * 512:((g % 2) + 1) * NC_ * 512],
                              xf[:, (g % 2) * NC_ * 512:((g % 2) + 1) * NC_ * 512]
                              ).then_inc(cvx, 1)
                for o in range(NC_):
                    j = PROJ_BASE[g] + o
                    v.wait_ge(mmp, j + 1)
                    v.tensor_copy(
                        _chunk3(kfr, 4 * 512)[:, o, (g % 4) * 512:(g % 4 + 1) * 512],
                        PP[j % 2][:]).then_inc(evpv, 1)
                if g in NSC_BASE:
                    for o in range(NC_):
                        j = PROJ_BASE[g] + NC_ + o
                        v.wait_ge(mmp, j + 1)
                        v.tensor_copy(_chunk3(qfb, 512)[:, o, :], PP[j % 2][:]
                                      ).then_inc(evpv, 1)
                    for kt in range(NKT):
                        sgi = NSC_BASE[g] + kt
                        v.wait_ge(mms, sgi + 1)
                        v.tensor_mul(scm[:, kt * 512:(kt + 1) * 512],
                                     PSC[sgi % 2][:],
                                     mrel[:, kt * 512:(kt + 1) * 512]).then_inc(evs, 1)

        @block.scalar
        def _(s):
            s.wait_ge(wcv, 1)
            s.activation(out=gct[:, 1:2], in_=gct[:, 0:1], func=AF.Ln)
            for rel in range(NKT):
                s.wait_ge(mio, rel + 1)
                s.activation(out=ex, in_=et,
                             func=AF.Exp, scale=gct[:, 1:2]).then_inc(mex, 1)
            n_y = 0
            for g in range(NGRP):
                base = PROJ_BASE[g] + (2 * NC_ if g in NSC_BASE else NC_)
                for i in range(NC_):
                    j = base + i
                    s.wait_ge(mmp, j + 1)
                    half = i % 2
                    s.copy(
                        _chunk3(vtr, D)[:, (g * 4 + i // 2) % NKT,
                                        half * 512:(half + 1) * 512],
                        PP[j % 2][:]).then_inc(evps, 1)
                if g in NSC_BASE:
                    for o in range(NC_):
                        yi = NY_BASE[g] + o
                        s.wait_ge(mmy, yi + 1)
                        if n_y >= 2:
                            s.wait_ge(so, 16 * (n_y - 1))
                        s.copy(yo[:, (n_y % 2) * 512:(n_y % 2 + 1) * 512],
                               PY[yi % 2][:]).then_inc(evy, 1)
                        n_y += 1

        @block.tensor
        def _(t):
            t.wait_ge(wcv, 6)
            t.wait_ge(mmk, NKT)
            jp = 0
            for g in range(NGRP):
                t.wait_ge(cvx, g + 1)
                xslab = _chunk3(xb[:, (g % 2) * NC_ * 512:((g % 2) + 1) * NC_ * 512], 512)
                for o in range(NC_):
                    wait_proj_free(t, jp)
                    for c in range(NC_):
                        inst = t.matmul(PP[jp % 2][:],
                                        _chunk3(wkb, D)[:, c, o * 128:(o + 1) * 128],
                                        xslab[:, c, :],
                                        start=(c == 0), stop=(c == NC_ - 1))
                    inst.then_inc(mmp, 1)
                    jp += 1
                if g in NSC_BASE:
                    for o in range(NC_):
                        wait_proj_free(t, jp)
                        for c in range(NC_):
                            inst = t.matmul(PP[jp % 2][:],
                                            _chunk3(wqb, D)[:, c, o * 128:(o + 1) * 128],
                                            xslab[:, c, :],
                                            start=(c == 0), stop=(c == NC_ - 1))
                        inst.then_inc(mmp, 1)
                        jp += 1
                for i in range(NC_):
                    tt_ = i // 2
                    half = i % 2
                    wait_proj_free(t, jp)
                    for c in range(NC_):
                        inst = t.matmul(PP[jp % 2][:],
                                        xslab[:, c, tt_ * 128:(tt_ + 1) * 128],
                                        _chunk3(wvb, D)[:, c, half * 512:(half + 1) * 512],
                                        start=(c == 0), stop=(c == NC_ - 1))
                    inst.then_inc(mmp, 1)
                    jp += 1
                if g in NSC_BASE:
                    for kt in range(NKT):
                        sgi = NSC_BASE[g] + kt
                        kj = g * 4 - 12 + kt
                        if sgi >= 2:
                            t.wait_ge(evs, sgi - 1)
                        rb = ((kj // 4) % 4) * 512 + (kj % 4) * 128
                        for c in range(NC_):
                            inst = t.matmul(
                                PSC[sgi % 2][:],
                                _chunk3(kfr, 4 * 512)[:, c, rb:rb + 128],
                                _chunk3(qfb, 512)[:, c, :],
                                start=(c == 0), stop=(c == NC_ - 1))
                        inst.then_inc(mms, 1)
                    for o in range(NC_):
                        yi = NY_BASE[g] + o
                        if yi >= 2:
                            t.wait_ge(evy, yi - 1)
                        t.wait_ge(evs, NSC_BASE[g] + NKT)
                        for kt in range(NKT):
                            kj = g * 4 - 12 + kt
                            inst = t.matmul(
                                PY[yi % 2][:],
                                _chunk3(vtr, D)[:, kj % NKT, o * 128:(o + 1) * 128],
                                scm[:, kt * 512:(kt + 1) * 512],
                                start=(kt == 0), stop=(kt == NKT - 1))
                        inst.then_inc(mmy, 1)
    return nc

MHALO = 512


def build_mamba_pre(skip_wx=False, skip_conv=False):
    nc = bass.Bass("TRN2", target_bir_lowering=False, debug=False)
    TT_ = MHALO + TLOC
    NGC = TT_ // 512
    NOC = TLOC // 512
    xn1T = nc.dram_tensor("xn1T", [D, TT_], F32, kind="ExternalInput").ap()
    w_in = nc.dram_tensor("w_in", [D, 2 * D], F32, kind="ExternalInput").ap()
    b_in = nc.dram_tensor("b_in", [128, 16], F32, kind="ExternalInput").ap()
    convw = nc.dram_tensor("convw", [128, 32], F32, kind="ExternalInput").ap()
    convb = nc.dram_tensor("convb", [128, 8], F32, kind="ExternalInput").ap()
    wx = nc.dram_tensor("wx", [D, 32], F32, kind="ExternalInput").ap()
    bx = nc.dram_tensor("bx", [32, 1], F32, kind="ExternalInput").ap()
    xcT = nc.dram_tensor("xcT", [D, TLOC], F32, kind="ExternalOutput").ap()
    bc = nc.dram_tensor("bc", [32, TLOC], F32, kind="ExternalOutput").ap()
    sgT = nc.dram_tensor("sgT", [D, TLOC], BF16, kind="ExternalOutput").ap()
    NC_ = D // 128
    NOB = 16
    with ExitStack() as es:
        sb, psum, sem = _ctx(nc, es)
        xf = sb("xf", [128, NC_ * 512], F32)
        xball = sb("xball", [128, NC_ * TT_], BF16)
        xsb = sb("xsb", [128, NC_ * TT_], BF16)
        wf = sb("wf", [128, 2 * D], F32)
        wbp = sb("wbp", [128, 2 * D], BF16)
        wxb = sb("wxb", [128, NC_ * 32], BF16)
        cwt = sb("cwt", [128, 32], F32)
        cbt = sb("cbt", [128, 8], F32)
        bnt = sb("bnt", [128, 16], F32)
        bxt = sb("bxt", [32, 1], F32)
        tmpA = sb("tmpA", [128, 512], F32)
        tmpB = sb("tmpB", [128, 512], F32)
        xcf = sb("xcf", [128, 2 * 512], F32)
        xcc = sb("xcc", [128, NC_ * 512], BF16)
        obg = sb("obg", [128, 4 * 512], BF16)
        bco = sb("bco", [32, 2 * 512], F32)
        pA = psum("pA", [128, 512])
        pB = psum("pB", [128, 512])
        pX = psum("pX", [32, 512])
        pY = psum("pY", [32, 512])
        ld = sem("ld")
        lcn = sem("lcn")
        wps = sem("wps")
        cvx = sem("cvx")
        cvw = sem("cvw")
        mm = sem("mm")
        evx = sem("evx")
        evg = sem("evg")
        sgo = sem("sgo")
        cvc = sem("cvc")
        scs = sem("scs")
        cvcc = sem("cvcc")
        mmx = sem("mmx")
        evb = sem("evb")
        soc = sem("soc")
        sob = sem("sob")
        block = es.enter_context(nc.Block())

        PP = [pA, pB]
        PX = [pX, pY]

        WIN_GROUPS = []
        for o in range(NOB):
            for gc in range(NGC):
                if o >= NC_ and gc == 0:
                    continue
                WIN_GROUPS.append((gc, o))
        NWIN = len(WIN_GROUPS)
        GROUPS_THRU_PANEL = []
        acc = 0
        for o in range(NOB):
            acc += NGC if o < NC_ else (NGC - 1)
            GROUPS_THRU_PANEL.append(acc)

        @block.sync
        def _(sync):
            sync.dma_start(out=bnt[:], in_=b_in[:, :]).then_inc(lcn, 16)
            sync.dma_start(out=cwt[:], in_=convw[:, :]).then_inc(lcn, 16)
            sync.dma_start(out=cbt[:], in_=convb[:, :]).then_inc(lcn, 16)
            sync.dma_start(out=bxt[:], in_=bx[:, :]).then_inc(lcn, 16)
            sync.dma_start(out=_chunk3(xf[:, 0:NC_ * 32], 32),
                           in_=wx[:, :].rearrange("(c p) f -> p c f", p=128)
                           ).then_inc(lcn, 16)
            for gc in range(NGC):
                if gc == 0:
                    sync.wait_ge(cvw, 1)
                else:
                    sync.wait_ge(cvx, gc)
                sync.dma_start(
                    out=_chunk3(xf[:, 0:NC_ * 512], 512),
                    in_=xn1T[:, gc * 512:(gc + 1) * 512].rearrange("(c p) f -> p c f", p=128),
                ).then_inc(ld, 16)
            n_o = 0
            for tc in range(NOC if not skip_conv else 0):
                for o in range(NC_):
                    sync.wait_ge(scs, tc * NC_ + o + 1)
                    sync.dma_start(out=xcT[o * 128:(o + 1) * 128, tc * 512:(tc + 1) * 512],
                                   in_=xcf[:, (n_o % 2) * 512:(n_o % 2 + 1) * 512]
                                   ).then_inc(soc, 16)
                    n_o += 1
                if not skip_wx:
                    sync.wait_ge(evb, tc + 1)
                    sync.dma_start(out=bc[:, tc * 512:(tc + 1) * 512],
                                   in_=bco[:, (tc % 2) * 512:(tc % 2 + 1) * 512]
                                   ).then_inc(sob, 16)
            if not (skip_wx or skip_conv):
                sync.wait_ge(sob, 16 * NOC)
            if not skip_conv:
                sync.wait_ge(soc, 16 * NOC * NC_)
            sync.wait_ge(sgo, 16 * NOC * NC_)

        @block.gpsimd
        def _(g_):
            def wdma(o):
                if o >= 2:
                    g_.wait_ge(cvw, o)
                g_.dma_start(out=_chunk3(wf[:, (o % 2) * D:(o % 2 + 1) * D], 128),
                             in_=_wpanel_src(w_in, o, D)).then_inc(wps, 16)

            def gstores(o, n_g0):
                for k in range(NGC - 1):
                    n_g = n_g0 + k
                    g_.wait_ge(evg, n_g + 1)
                    g_.dma_start(out=sgT[(o - NC_) * 128:(o - NC_ + 1) * 128,
                                         k * 512:(k + 1) * 512],
                                 in_=obg[:, (n_g % 4) * 512:(n_g % 4 + 1) * 512]
                                 ).then_inc(sgo, 16)
            for o in range(NC_ + 2):
                wdma(o)
            for o in range(NC_ + 2, NOB):
                wdma(o)
                gstores(o - 2, (o - 2 - NC_) * (NGC - 1))
            gstores(NOB - 2, (NOB - 2 - NC_) * (NGC - 1))
            gstores(NOB - 1, (NOB - 1 - NC_) * (NGC - 1))

        @block.vector
        def _(v):
            v.wait_ge(lcn, 16 * 5)
            v.tensor_copy(_chunk3(wxb, 32), _chunk3(xf[:, 0:NC_ * 32], 32)).then_inc(cvw, 1)
            for gc in range(NGC):
                v.wait_ge(ld, 16 * (gc + 1))
                v.tensor_copy(
                    _chunk3(xball, TT_)[:, :, gc * 512:(gc + 1) * 512],
                    _chunk3(xf[:, 0:NC_ * 512], 512),
                ).then_inc(cvx, 1)
            for o in range(NOB):
                v.wait_ge(wps, 16 * (o + 1))
                if o >= 2:
                    v.wait_ge(mm, GROUPS_THRU_PANEL[o - 2])
                v.tensor_copy(wbp[:, (o % 2) * D:(o % 2 + 1) * D],
                              wf[:, (o % 2) * D:(o % 2 + 1) * D]).then_inc(cvw, 1)
            for tc in range(NOC if not skip_conv else 0):
                for o in range(NC_):
                    v.wait_ge(evx, o * NGC + tc + 2)
                    base = o * TT_ + (tc + 1) * 512
                    xs = xsb[:]
                    v.tensor_scalar_mul(tmpA[:], xsb[:, base - 3:base - 3 + 512],
                                        cwt[:, 4 * o:4 * o + 1])
                    v.scalar_tensor_tensor(tmpB[:], xsb[:, base - 2:base - 2 + 512],
                                           cwt[:, 4 * o + 1:4 * o + 2], tmpA[:],
                                           ALU.mult, ALU.add)
                    v.scalar_tensor_tensor(tmpA[:], xsb[:, base - 1:base - 1 + 512],
                                           cwt[:, 4 * o + 2:4 * o + 3], tmpB[:],
                                           ALU.mult, ALU.add)
                    v.scalar_tensor_tensor(tmpB[:], xsb[:, base:base + 512],
                                           cwt[:, 4 * o + 3:4 * o + 4], tmpA[:],
                                           ALU.mult, ALU.add).then_inc(cvc, 1)
                    v.wait_ge(scs, tc * NC_ + o + 1)
                    v.tensor_copy(xcc[:, o * 512:(o + 1) * 512],
                                  xcf[:, ((tc * NC_ + o) % 2) * 512:
                                      ((tc * NC_ + o) % 2 + 1) * 512]).then_inc(cvcc, 1)

        @block.scalar
        def _(s):
            n_x = 0
            n_g = 0
            for gi, (gc, o) in enumerate(WIN_GROUPS):
                s.wait_ge(mm, gi + 1)
                if o < NC_:
                    s.activation(out=_chunk3(xsb, TT_)[:, o, gc * 512:(gc + 1) * 512],
                                 in_=PP[gi % 2][:], func=AF.Identity,
                                 bias=bnt[:, o:o + 1]).then_inc(evx, 1)
                    n_x += 1
                else:
                    if n_g >= 4:
                        s.wait_ge(sgo, 16 * (n_g - 3))
                    s.activation(out=obg[:, (n_g % 4) * 512:(n_g % 4 + 1) * 512],
                                 in_=PP[gi % 2][:], func=AF.Silu,
                                 bias=bnt[:, o:o + 1]).then_inc(evg, 1)
                    n_g += 1
            for tc in range(NOC if not skip_conv else 0):
                for o in range(NC_):
                    j = tc * NC_ + o
                    s.wait_ge(cvc, j + 1)
                    if j >= 2:
                        s.wait_ge(soc, 16 * (j - 1))
                    s.activation(out=xcf[:, (j % 2) * 512:(j % 2 + 1) * 512],
                                 in_=tmpB[:], func=AF.Silu,
                                 bias=cbt[:, o:o + 1]).then_inc(scs, 1)
                if skip_wx:
                    continue
                s.wait_ge(mmx, tc + 1)
                if tc >= 2:
                    s.wait_ge(sob, 16 * (tc - 1))
                s.activation(out=bco[:, (tc % 2) * 512:(tc % 2 + 1) * 512],
                             in_=PX[tc % 2][0:32, :], func=AF.Identity,
                             bias=bxt[0:32, 0:1]).then_inc(evb, 1)

        @block.tensor
        def _(t):
            for gi, (gc, o) in enumerate(WIN_GROUPS):
                if gi == 0:
                    t.wait_ge(cvw, 2)
                t.wait_ge(cvx, gc + 1)
                t.wait_ge(cvw, o + 2)
                if gi >= 2:
                    prev = gi - 2
                    nx_prev = len([1 for (g2, o2) in WIN_GROUPS[:prev + 1] if o2 < NC_])
                    ng_prev = prev + 1 - nx_prev
                    t.wait_ge(evx, nx_prev) if WIN_GROUPS[prev][1] < NC_ else \
                        t.wait_ge(evg, ng_prev)
                xslab = _chunk3(xball, TT_)
                for c in range(NC_):
                    inst = t.matmul(PP[gi % 2][:],
                                    wbp[:, (o % 2) * D + c * 128:(o % 2) * D + (c + 1) * 128],
                                    xslab[:, c, gc * 512:(gc + 1) * 512],
                                    start=(c == 0), stop=(c == NC_ - 1))
                inst.then_inc(mm, 1)
            for tc in range(NOC if not (skip_wx or skip_conv) else 0):
                t.wait_ge(cvcc, (tc + 1) * NC_)
                if tc >= 2:
                    t.wait_ge(evb, tc - 1)
                for c in range(NC_):
                    inst = t.matmul(PX[tc % 2][0:32, :],
                                    _chunk3(wxb, 32)[:, c, :],
                                    xcc[:, c * 512:(c + 1) * 512],
                                    start=(c == 0), stop=(c == NC_ - 1))
                inst.then_inc(mmx, 1)
    return nc

DHALF = 512
NST = 16


def build_scan():
    nc = bass.Bass("TRN2", target_bir_lowering=False, debug=False)
    xch = nc.dram_tensor("xch", [DHALF, L], F32, kind="ExternalInput").ap()
    bcm = nc.dram_tensor("bcm", [2 * NST, L], F32, kind="ExternalInput").ap()
    alog = nc.dram_tensor("alog", [128, 64], F32, kind="ExternalInput").ap()
    dpar = nc.dram_tensor("dpar", [128, 4], F32, kind="ExternalInput").ap()
    sgh = nc.dram_tensor("sgh", [DHALF, L], F32, kind="ExternalInput").ap()
    sel = nc.dram_tensor("sel", [128, 4 * 128], F32, kind="ExternalInput").ap()
    yh = nc.dram_tensor("yh", [DHALF, L], BF16, kind="ExternalOutput").ap()
    NDB = DHALF // 128
    NTS = 8
    TS = L // NTS
    NCT = 16
    NTC = TS // 512
    with ExitStack() as es:
        sb, psum, sem = _ctx(nc, es)
        t1a = sb("t1a", [128, NCT * TS], F32)
        bcre = sb("bcre", [128, 8 * TS], F32)
        hfull = sb("hfull", [128, TS], F32)
        ubv = sb("ubv", [128, TS], F32)
        ub = sb("ub", [128, 3 * TS], F32)
        xcr = sb("xcr", [128, 2 * TS], F32)
        xcd = sb("xcd", [128, 2 * TS], F32)
        sgd = sb("sgd", [128, 2 * TS], F32)
        aall = sb("aall", [128, 64], F32)
        atmp = sb("atmp", [128, 64], F32)
        dpt = sb("dpt", [128, 4], F32)
        self_ = sb("self_", [128, 4 * 128], F32)
        state = sb("state", [128, 64], F32)
        yt = sb("yt", [128, 2 * 512], F32)
        yob = sb("yob", [128, 2 * 512], BF16)
        pY = [psum("pY0", [128, 512]), psum("pY1", [128, 512])]
        ldp = sem("ldp")
        aex = sem("aex")
        ldbc = sem("ldbc")
        ldr = sem("ldr")
        ldd = sem("ldd")
        uok = sem("uok")
        cvu = sem("cvu")
        mmy = sem("mmy")
        evy = sem("evy")
        ssc = sem("ssc")
        so = sem("so")
        block = es.enter_context(nc.Block())

        SEGS = [(ts, db) for ts in range(NTS) for db in range(NDB)]
        NSEG = len(SEGS)

        @block.sync
        def _(sync):
            sync.dma_start(out=aall[:], in_=alog[:, :]).then_inc(ldp, 16)
            sync.dma_start(out=dpt[:], in_=dpar[:, :]).then_inc(ldp, 16)
            sync.dma_start(out=self_[:], in_=sel[:, :]).then_inc(ldp, 16)
            nct = 0
            for ts in range(NTS):
                if ts >= 1:
                    sync.wait_ge(cvu, ts * NDB * NCT)
                for ng in range(4):
                    sync.dma_start(
                        out=bcre[:, ng * TS:(ng + 1) * TS],
                        in_=bcm[ng * 4:(ng + 1) * 4, ts * TS:(ts + 1) * TS]
                            .unsqueeze(0).to_broadcast([32, 4, TS])).then_inc(ldbc, 16)
                    sync.dma_start(
                        out=bcre[:, (4 + ng) * TS:(5 + ng) * TS],
                        in_=bcm[NST + ng * 4:NST + (ng + 1) * 4, ts * TS:(ts + 1) * TS]
                            .unsqueeze(0).to_broadcast([32, 4, TS])).then_inc(ldbc, 16)
                for db in range(NDB):
                    for ct in range(NCT):
                        dg = ct // 4
                        par = nct % 2
                        if nct >= 2:
                            sync.wait_ge(uok, nct - 1)
                        if 2 <= nct < 28:
                            sync.wait_ge(cvu, nct - 1)
                        sync.dma_start(
                            out=xcr[:, par * TS:(par + 1) * TS],
                            in_=xch[db * 128 + dg * 32:db * 128 + (dg + 1) * 32,
                                    ts * TS:(ts + 1) * TS]
                                .unsqueeze(1).to_broadcast([32, 4, TS])).then_inc(ldr, 16)
                        nct += 1

        @block.gpsimd
        def _(g_):
            nct = 0
            for si, (ts, db) in enumerate(SEGS):
                for ct in range(NCT):
                    ng = ct % 4
                    par = nct % 2
                    g_.wait_ge(ldr, 16 * (nct + 1))
                    g_.wait_ge(ldbc, 16 * 8 * (ts + 1))
                    if nct >= 3:
                        g_.wait_ge(cvu, nct - 2)
                    g_.tensor_mul(ub[:, (nct % 3) * TS:(nct % 3 + 1) * TS],
                                  xcr[:, par * TS:(par + 1) * TS],
                                  bcre[:, ng * TS:(ng + 1) * TS]).then_inc(uok, 1)
                    nct += 1
            g_.wait_ge(so, 16 * NSEG * NTC)

        @block.scalar
        def _(s):
            s.wait_ge(ldp, 16)
            s.activation(out=atmp[:], in_=aall[:], func=AF.Exp)
            s.activation(out=aall[:], in_=atmp[:], func=AF.Exp, scale=-1.0).then_inc(aex, 1)
            s.copy(yt[:, 0:64], aall[:]).then_inc(aex, 1)

            def ystores(sj):
                ts_j, db_j = SEGS[sj]
                for tc in range(NTC):
                    n_y = sj * NTC + tc
                    s.wait_ge(evy, n_y + 1)
                    s.dma_start(
                        out=yh[db_j * 128:(db_j + 1) * 128,
                               ts_j * TS + tc * 512:ts_j * TS + (tc + 1) * 512],
                        in_=yob[:, (n_y % 2) * 512:(n_y % 2 + 1) * 512]).then_inc(so, 16)
            for si, (ts, db) in enumerate(SEGS):
                par_s = si % 2
                if si >= 2:
                    s.wait_ge(evy, (si - 1) * NTC)
                s.dma_start(out=xcd[:, par_s * TS:(par_s + 1) * TS],
                            in_=xch[db * 128:(db + 1) * 128, ts * TS:(ts + 1) * TS]
                            ).then_inc(ldd, 16)
                s.dma_start(out=sgd[:, par_s * TS:(par_s + 1) * TS],
                            in_=sgh[db * 128:(db + 1) * 128, ts * TS:(ts + 1) * TS]
                            ).then_inc(ldd, 16)
                if si >= 1:
                    ystores(si - 1)
            ystores(NSEG - 1)

        @block.vector
        def _(v):
            v.wait_ge(aex, 2)
            v.memset(yt[:], 0.0)
            v.memset(yt[:], 0.0)
            nct = 0
            n_y = 0
            for si, (ts, db) in enumerate(SEGS):
                par_s = si % 2
                for ct in range(NCT):
                    ng_ = ct % 4
                    par = nct % 2
                    v.wait_ge(uok, nct + 1)
                    if nct < 24:
                        v.wait_ge(ldr, 16 * (nct + 1))
                        v.wait_ge(ldbc, 16 * 8)
                        v.tensor_mul(ubv[:], xcr[:, par * TS:(par + 1) * TS],
                                     bcre[:, ng_ * TS:(ng_ + 1) * TS])
                    if ts == 0:
                        init = 0.0
                    else:
                        init = state[:, db * NCT + ct:db * NCT + ct + 1]
                    if ct == 0 and si >= 1:
                        v.wait_ge(mmy, si * NTC)
                    u_src = ubv[:] if nct < 24 else ub[:, (nct % 3) * TS:(nct % 3 + 1) * TS]
                    v.tensor_tensor_scan(
                        hfull[:], aall[:, db * NCT + ct:db * NCT + ct + 1]
                        .to_broadcast([128, TS]), u_src,
                        init, ALU.mult, ALU.add).then_inc(ssc, 1)
                    v.tensor_mul(t1a[:, ct * TS:(ct + 1) * TS], hfull[:],
                                 bcre[:, (4 + ct % 4) * TS:(5 + ct % 4) * TS])
                    v.wait_ge(ssc, nct + 1)
                    v.tensor_copy(state[:, db * NCT + ct:db * NCT + ct + 1],
                                  hfull[:, TS - 1:TS]).then_inc(cvu, 1)
                    nct += 1
                v.wait_ge(ldd, 32 * (si + 1))
                for tc in range(NTC):
                    par = n_y % 2
                    v.wait_ge(mmy, n_y + 1)
                    if n_y >= 2:
                        v.wait_ge(so, 16 * (n_y - 1))
                    v.scalar_tensor_tensor(
                        yt[:, par * 512:(par + 1) * 512],
                        xcd[:, par_s * TS + tc * 512:par_s * TS + (tc + 1) * 512],
                        dpt[:, db:db + 1], pY[par][:], ALU.mult, ALU.add)
                    v.tensor_mul(
                        yob[:, par * 512:(par + 1) * 512],
                        yt[:, par * 512:(par + 1) * 512],
                        sgd[:, par_s * TS + tc * 512:par_s * TS + (tc + 1) * 512]
                    ).then_inc(evy, 1)
                    n_y += 1

        @block.tensor
        def _(t):
            t.wait_ge(ldp, 48)
            n_y = 0
            for si, (ts, db) in enumerate(SEGS):
                t.wait_ge(cvu, (si + 1) * NCT)
                for tc in range(NTC):
                    par = n_y % 2
                    if n_y >= 2:
                        t.wait_ge(evy, n_y - 1)
                    for dg in range(4):
                        for ng in range(4):
                            ct = dg * 4 + ng
                            inst = t.matmul(
                                pY[par][:],
                                self_[:, dg * 128:(dg + 1) * 128],
                                t1a[:, ct * TS + tc * 512:ct * TS + (tc + 1) * 512],
                                start=(dg == 0 and ng == 0), stop=(dg == 3 and ng == 3))
                    inst.then_inc(mmy, 1)
                    n_y += 1

    return nc


def build_wout():
    nc = bass.Bass("TRN2", target_bir_lowering=False, debug=False)
    yT = nc.dram_tensor("yT", [D, TLOC], BF16, kind="ExternalInput").ap()
    w_out = nc.dram_tensor("w_out", [D, D], F32, kind="ExternalInput").ap()
    b_out = nc.dram_tensor("b_out", [128, 8], F32, kind="ExternalInput").ap()
    s1T = nc.dram_tensor("s1T", [D, TLOC], F32, kind="ExternalOutput").ap()
    NC_ = D // 128
    NTC = TLOC // 512
    with ExitStack() as es:
        sb, psum, sem = _ctx(nc, es)
        xb = sb("xb", [128, NC_ * TLOC], BF16)
        wf = sb("wf", [128, NC_ * 512], F32)
        wb = sb("wb", [128, NC_ * D], BF16)
        bt = sb("bt", [128, 8], F32)
        ob = sb("ob", [128, 2 * 512], F32)
        PS = [psum("p0", [128, 512]), psum("p1", [128, 512])]
        ld = sem("ld")
        wls = sem("wls")
        cvw = sem("cvw")
        mm = sem("mm")
        ev = sem("ev")
        so = sem("so")
        block = es.enter_context(nc.Block())

        @block.sync
        def _(sync):
            sync.dma_start(out=bt[:], in_=b_out[:, :]).then_inc(ld, 16)
            for half in range(2):
                if half >= 1:
                    sync.wait_ge(cvw, half)
                sync.dma_start(
                    out=_chunk3(wf, 512),
                    in_=w_out[:, half * 512:(half + 1) * 512]
                        .rearrange("(c p) f -> p c f", p=128)).then_inc(wls, 16)
            sync.dma_start(out=_chunk3(xb, TLOC),
                           in_=yT[:, :].rearrange("(c p) f -> p c f", p=128)
                           ).then_inc(ld, 16)

        @block.vector
        def _(v):
            for half in range(2):
                v.wait_ge(wls, 16 * (half + 1))
                v.tensor_copy(_chunk3(wb, D)[:, :, half * 512:(half + 1) * 512],
                              _chunk3(wf, 512)).then_inc(cvw, 1)

        @block.tensor
        def _(t):
            t.wait_ge(cvw, 2)
            t.wait_ge(ld, 32)
            g_idx = 0
            for o in range(NC_):
                for tc in range(NTC):
                    par = g_idx % 2
                    if g_idx >= 2:
                        t.wait_ge(ev, g_idx - 1)
                    for c in range(NC_):
                        inst = t.matmul(PS[par][:],
                                        _chunk3(wb, D)[:, c, o * 128:(o + 1) * 128],
                                        _chunk3(xb, TLOC)[:, c, tc * 512:(tc + 1) * 512],
                                        start=(c == 0), stop=(c == NC_ - 1))
                    inst.then_inc(mm, 1)
                    g_idx += 1

        @block.scalar
        def _(s):
            g_idx = 0
            for o in range(NC_):
                for tc in range(NTC):
                    par = g_idx % 2
                    s.wait_ge(mm, g_idx + 1)
                    if g_idx >= 2:
                        s.wait_ge(so, 16 * (g_idx - 1))
                    s.activation(out=ob[:, par * 512:(par + 1) * 512], in_=PS[par][:],
                                 func=AF.Identity, bias=bt[:, o:o + 1]).then_inc(ev, 1)
                    g_idx += 1

        @block.gpsimd
        def _(g_):
            g_idx = 0
            for o in range(NC_):
                for tc in range(NTC):
                    par = g_idx % 2
                    g_.wait_ge(ev, g_idx + 1)
                    g_.dma_start(out=s1T[o * 128:(o + 1) * 128, tc * 512:(tc + 1) * 512],
                                 in_=ob[:, par * 512:(par + 1) * 512]).then_inc(so, 16)
                    g_idx += 1
            g_.wait_ge(so, 16 * NC_ * NTC)
    return nc

def build_mod_a():
    nc = bass.Bass("TRN2", target_bir_lowering=False, debug=False)
    xn1T = nc.dram_tensor("xn1T", [D, TLOC], F32, kind="ExternalInput").ap()
    w1 = nc.dram_tensor("w1", [D, DHID], F32, kind="ExternalInput").ap()
    b1 = nc.dram_tensor("b1", [128, DHID // 128], F32, kind="ExternalInput").ap()
    wrt = nc.dram_tensor("wrt", [128, 8], F32, kind="ExternalInput").ap()
    hmod = nc.dram_tensor("hmod", [DHID, TLOC], BF16, kind="ExternalOutput").ap()
    sc = nc.dram_tensor("sc", [1, TLOC], F32, kind="ExternalOutput").ap()
    NK = D // 128
    NO = DHID // 128
    NTC = TLOC // 512
    NG = NO * NTC
    NB = 3
    C0 = 0.044715
    C1 = 0.7978845608028654
    with ExitStack() as es:
        sb, psum, sem = _ctx(nc, es)
        xf = sb("xf", [128, 2 * NK * 512], F32)
        xb = sb("xb", [128, NK * TLOC], BF16)
        wrf = sb("wrf", [128, NK], F32)
        wrb = sb("wrb", [128, NK], BF16)
        wf = sb("wf", [128, NB * D], F32)
        wb = sb("wb", [128, NB * D], BF16)
        bt = sb("bt", [128, DHID // 128], F32)
        vv_ = sb("vv_", [128, 2 * 512], F32)
        v2_ = sb("v2_", [128, 2 * 512], F32)
        e1_ = sb("e1_", [128, 512], F32)
        e2_ = sb("e2_", [128, 2 * 512], F32)
        t_ = sb("t_", [128, 2 * 512], F32)
        ob = sb("ob", [128, 2 * 512], BF16)
        scr = sb("scr", [1, TLOC], F32)
        PS = [psum("p0", [128, 512]), psum("p1", [128, 512])]
        pS = psum("pS", [1, 512])
        ld = sem("ld")
        wps = sem("wps")
        cvx = sem("cvx")
        cvw = sem("cvw")
        mm = sem("mm")
        mms = sem("mms")
        evs = sem("evs")
        sca = sem("sca")
        vea = sem("vea")
        poa = sem("poa")
        ev = sem("ev")
        so = sem("so")
        sos = sem("sos")
        block = es.enter_context(nc.Block())

        @block.sync
        def _(sync):
            sync.dma_start(out=bt[:], in_=b1[:, :]).then_inc(ld, 16)
            sync.dma_start(out=wrf[:], in_=wrt[:, :]).then_inc(ld, 16)
            for tc in range(NTC):
                if tc >= 2:
                    sync.wait_ge(cvx, tc - 1)
                    sync.wait_ge(mms, tc - 1)
                sync.dma_start(
                    out=_chunk3(xf[:, (tc % 2) * NK * 512:(tc % 2 + 1) * NK * 512], 512),
                    in_=xn1T[:, tc * 512:(tc + 1) * 512].rearrange("(c p) f -> p c f", p=128),
                ).then_inc(ld, 16)
            sync.wait_ge(evs, NTC)
            sync.dma_start(out=sc[:, :], in_=scr[:]).then_inc(sos, 16)
            sync.wait_ge(sos, 16)

        @block.gpsimd
        def _(g_):
            def wdma(o):
                if o >= NB:
                    g_.wait_ge(cvw, o - NB + 1)
                g_.dma_start(out=_chunk3(wf[:, (o % NB) * D:(o % NB + 1) * D], 128),
                             in_=_wpanel_src(w1, o, D)).then_inc(wps, 16)
            for o in range(NB):
                wdma(o)
            g_idx = 0
            for o in range(NO):
                for tc in range(NTC):
                    par = g_idx % 2
                    g_.wait_ge(vea, g_idx + 1)
                    if g_idx >= 2:
                        g_.wait_ge(sca, 3 * g_idx - 1)
                    g_.tensor_mul(e2_[:, par * 512:(par + 1) * 512],
                                  e1_[:],
                                  vv_[:, par * 512:(par + 1) * 512]).then_inc(poa, 1)
                    if g_idx >= 2:
                        j = g_idx - 2
                        g_.wait_ge(ev, j + 1)
                        oj, tcj = j // NTC, j % NTC
                        g_.dma_start(out=hmod[oj * 128:(oj + 1) * 128,
                                              tcj * 512:(tcj + 1) * 512],
                                     in_=ob[:, (j % 2) * 512:(j % 2 + 1) * 512]
                                     ).then_inc(so, 16)
                    g_idx += 1
                if o + NB < NO:
                    wdma(o + NB)
            for j in range(NG - 2, NG):
                g_.wait_ge(ev, j + 1)
                oj, tcj = j // NTC, j % NTC
                g_.dma_start(out=hmod[oj * 128:(oj + 1) * 128,
                                      tcj * 512:(tcj + 1) * 512],
                             in_=ob[:, (j % 2) * 512:(j % 2 + 1) * 512]).then_inc(so, 16)
            g_.wait_ge(so, 16 * NG)

        @block.vector
        def _(v):
            v.wait_ge(ld, 32)
            v.tensor_copy(wrb[:], wrf[:]).then_inc(cvw, 1)
            for tc in range(NTC):
                v.wait_ge(ld, 32 + 16 * (tc + 1))
                v.tensor_copy(
                    _chunk3(xb, TLOC)[:, :, tc * 512:(tc + 1) * 512],
                    _chunk3(xf[:, (tc % 2) * NK * 512:(tc % 2 + 1) * NK * 512], 512)
                ).then_inc(cvx, 1)
                if tc >= 1:
                    v.wait_ge(mms, tc)
                    v.tensor_copy(scr[0:1, (tc - 1) * 512:tc * 512],
                                  pS[0:1, :]).then_inc(evs, 1)
            v.wait_ge(mms, NTC)
            v.tensor_copy(scr[0:1, (NTC - 1) * 512:NTC * 512],
                          pS[0:1, :]).then_inc(evs, 1)
            def castw(o):
                v.wait_ge(wps, 16 * (o + 1))
                v.tensor_copy(wb[:, (o % NB) * D:(o % NB + 1) * D],
                              wf[:, (o % NB) * D:(o % NB + 1) * D]).then_inc(cvw, 1)
            for o in range(min(NB, NO)):
                castw(o)
            g_idx = 0
            for o in range(NO):
                for tc in range(NTC):
                    par = g_idx % 2
                    v.wait_ge(sca, 2 if g_idx == 0 else 3 * g_idx + 1)
                    if g_idx >= 1:
                        v.wait_ge(poa, g_idx)
                    v.tensor_scalar(e1_[:], v2_[:, par * 512:(par + 1) * 512],
                                    C0, 1.0, ALU.mult, ALU.add).then_inc(vea, 1)
                    v.wait_ge(sca, 3 * NG if g_idx == NG - 1
                              else (5 if g_idx == 0 else 3 * g_idx + 5))
                    if g_idx >= 2:
                        v.wait_ge(so, 16 * (g_idx - 1))
                    v.tensor_scalar(t_[:, 512 + par * 512:512 + (par + 1) * 512]
                                    if False else t_[:, par * 512:(par + 1) * 512],
                                    t_[:, par * 512:(par + 1) * 512],
                                    0.5, 0.5, ALU.mult, ALU.add)
                    v.tensor_mul(ob[:, par * 512:(par + 1) * 512],
                                 t_[:, par * 512:(par + 1) * 512],
                                 vv_[:, par * 512:(par + 1) * 512]).then_inc(ev, 1)
                    g_idx += 1
                if o + NB < NO:
                    castw(o + NB)

        @block.tensor
        def _(t):
            for tc in range(NTC):
                t.wait_ge(ld, 32 + 16 * (tc + 1))
                for c in range(NK):
                    inst = t.matmul(pS[0:1, :],
                                    wrf[:, c:c + 1],
                                    _chunk3(xf[:, (tc % 2) * NK * 512:(tc % 2 + 1) * NK * 512],
                                            512)[:, c, :],
                                    start=(c == 0), stop=(c == NK - 1))
                inst.then_inc(mms, 1)
                t.wait_ge(evs, tc + 1)
            t.wait_ge(cvx, NTC)
            g_idx = 0
            for o in range(NO):
                t.wait_ge(cvw, o + 2)
                for tc in range(NTC):
                    par = g_idx % 2
                    if g_idx >= 2:
                        t.wait_ge(sca, 1 if g_idx == 2 else 3 * (g_idx - 2))
                    for k in range(NK):
                        inst = t.matmul(PS[par][:],
                                        wb[:, (o % NB) * D + k * 128:(o % NB) * D + (k + 1) * 128],
                                        _chunk3(xb, TLOC)[:, k, tc * 512:(tc + 1) * 512],
                                        start=(k == 0), stop=(k == NK - 1))
                    inst.then_inc(mm, 1)
                    g_idx += 1

        @block.scalar
        def _(s):
            def vsq(g_idx):
                par = g_idx % 2
                o = g_idx // NTC
                s.wait_ge(mm, g_idx + 1)
                if g_idx >= 2:
                    s.wait_ge(ev, g_idx - 1)
                s.activation(out=vv_[:, par * 512:(par + 1) * 512], in_=PS[par][:],
                             func=AF.Identity, bias=bt[:, o:o + 1]).then_inc(sca, 1)
                s.square(v2_[:, par * 512:(par + 1) * 512],
                         vv_[:, par * 512:(par + 1) * 512]).then_inc(sca, 1)

            def tanh(g_idx):
                par = g_idx % 2
                s.wait_ge(poa, g_idx + 1)
                s.activation(out=t_[:, par * 512:(par + 1) * 512],
                             in_=e2_[:, par * 512:(par + 1) * 512],
                             func=AF.Tanh, scale=C1).then_inc(sca, 1)
            vsq(0)
            for g_idx in range(1, NG):
                vsq(g_idx)
                tanh(g_idx - 1)
            tanh(NG - 1)

    return nc

KSEL = 4096.0
NITER_BIS = 80


def build_mod_b():
    nc = bass.Bass("TRN2", target_bir_lowering=False, debug=False)
    sc_all = nc.dram_tensor("sc_all", [128, 64], F32, kind="ExternalInput").ap()
    sc_own = nc.dram_tensor("sc_own", [1, TLOC], F32, kind="ExternalInput").ap()
    hmod = nc.dram_tensor("hmod", [DHID, TLOC], BF16, kind="ExternalInput").ap()
    w2 = nc.dram_tensor("w2", [DHID, D], F32, kind="ExternalInput").ap()
    b2 = nc.dram_tensor("b2", [128, 8], F32, kind="ExternalInput").ap()
    xn1T = nc.dram_tensor("xn1T", [D, TLOC], F32, kind="ExternalInput").ap()
    xl1T = nc.dram_tensor("xl1T", [D, TLOC], F32, kind="ExternalInput").ap()
    s1T = nc.dram_tensor("s1T", [D, TLOC], F32, kind="ExternalInput").ap()
    outT = nc.dram_tensor("outT", [D, TLOC], F32, kind="ExternalOutput").ap()
    NK = DHID // 128
    NO = D // 128
    NQ = 4
    TH = TLOC // NQ
    NTC = TH // 512
    NJ = NQ * NO
    NG = NJ * NTC
    with ExitStack() as es:
        sb, psum, sem = _ctx(nc, es)
        xb = sb("xb", [128, NK * TH], BF16)
        wf = sb("wf", [128, 2 * NK * 128], F32)
        wb = sb("wb", [128, 2 * NK * 128], BF16)
        bt = sb("bt", [128, 8], F32)
        sat = sb("sat", [128, 64], F32)
        sot = sb("sot", [1, TLOC], F32)
        sgm = sb("sgm", [1, TLOC], BF16)
        wrow = sb("wrow", [1, TLOC], BF16)
        wbc = sb("wbc", [128, TLOC], BF16)
        ones1 = sb("ones1", [1, 128], F32)
        ones1b = sb("ones1b", [1, 128], BF16)
        onesc = sb("onesc", [128, 1], F32)
        bis = sb("bis", [128, 12], F32)
        cg = sb("cg", [128, 64], F32)
        u1 = sb("u1", [128, 2 * 512], F32)
        m2 = sb("m2", [128, 2 * 512], F32)
        g1 = sb("g1", [128, 2 * 512], F32)
        xn1t = sb("xn1t", [128, 2 * 512], F32)
        s1t = sb("s1t", [128, 2 * 512], F32)
        xl1t = sb("xl1t", [128, 2 * 512], F32)
        ot = sb("ot", [128, 2 * 512], F32)
        PS = [psum("p0", [128, 512]), psum("p1", [128, 512])]
        pmid = psum("pmid", [128, 1])
        pcnt = psum("pcnt", [1, 1])
        pw = psum("pw", [128, 512])
        ld = sem("ld")
        wps = sem("wps")
        cvw = sem("cvw")
        vsf = sem("vsf")
        tvs = sem("tvs")
        pvs = sem("pvs")
        scg = sem("scg")
        lsx = sem("lsx")
        mm = sem("mm")
        ev = sem("ev")
        vc = sem("vc")
        so = sem("so")
        block = es.enter_context(nc.Block())

        NTC8 = TLOC // 512

        def group_pos(g_idx):
            q = g_idx // (NO * NTC)
            o = (g_idx // NTC) % NO
            tc = g_idx % NTC
            return q, o, tc

        @block.sync
        def _(sync):
            sync.dma_start(out=sat[:], in_=sc_all[:, :]).then_inc(ld, 16)
            sync.dma_start(out=sot[:], in_=sc_own[:, :]).then_inc(ld, 16)
            sync.dma_start(out=bt[:], in_=b2[:, :]).then_inc(ld, 16)
            for q in range(NQ):
                if q >= 1:
                    sync.wait_ge(mm, q * NO * NTC)
                sync.dma_start(
                    out=_chunk3(xb, TH),
                    in_=hmod[:, q * TH:(q + 1) * TH]
                        .rearrange("(c p) f -> p c f", p=128)).then_inc(ld, 16)

        @block.gpsimd
        def _(g_):
            def wdma(j):
                if j >= 2:
                    g_.wait_ge(cvw, j - 1)
                g_.dma_start(out=_chunk3(wf[:, (j % 2) * DHID:(j % 2 + 1) * DHID], 128),
                             in_=_wpanel_src(w2, j % NO, DHID)).then_inc(wps, 16)
            wdma(0)
            wdma(1)
            g_idx = 0
            for j in range(NJ):
                for tc in range(NTC):
                    par = g_idx % 2
                    g_.wait_ge(vc, g_idx + 1)
                    q_, o, _tc = group_pos(g_idx)
                    t0 = q_ * TH + tc * 512
                    g_.dma_start(out=outT[o * 128:(o + 1) * 128, t0:t0 + 512],
                                 in_=ot[:, par * 512:(par + 1) * 512]).then_inc(so, 16)
                    g_idx += 1
                if j + 2 < NJ:
                    wdma(j + 2)
            g_.wait_ge(so, 16 * NG)

        vcnt = [0]

        @block.vector
        def _(v):
            def ss(inst):
                vcnt[0] += 1
                inst.then_inc(vsf, 1)
                v.wait_ge(vsf, vcnt[0])

            v.wait_ge(ld, 48)
            v.memset(ones1[:], 1.0)
            v.memset(ones1b[:], 1.0)
            v.memset(onesc[:], 1.0)
            v.memset(bis[0:1, 0:1], -1.0e4)
            ss(v.memset(bis[0:1, 1:2], 1.0e4))
            for it in range(NITER_BIS):
                ss(v.tensor_sub(bis[0:1, 4:5], bis[0:1, 1:2], bis[0:1, 0:1]))
                ss(v.scalar_tensor_tensor(bis[0:1, 2:3], bis[0:1, 4:5], 0.5,
                                          bis[0:1, 0:1], ALU.mult, ALU.add))
                v.tensor_copy(bis[0:1, 3:4], bis[0:1, 2:3]).then_inc(tvs, 1)
                v.wait_ge(pvs, 2 * it + 1)
                ss(v.tensor_copy(bis[:, 6:7], pmid[:, 0:1]))
                v.tensor_scalar(cg[:], sat[:], bis[:, 6:7], None, ALU.is_gt)
                ss(v.reduce_sum(bis[:, 7:8], cg[:], axis=AX.X))
                v.tensor_copy(bis[:, 8:9], bis[:, 7:8]).then_inc(tvs, 1)
                v.wait_ge(pvs, 2 * it + 2)
                ss(v.tensor_scalar(bis[0:1, 3:4], pcnt[0:1, 0:1], KSEL, None,
                                   ALU.is_ge))
                ss(v.tensor_scalar(bis[0:1, 9:10], bis[0:1, 3:4], -1.0, 1.0,
                                   ALU.mult, ALU.add))
                ss(v.tensor_scalar_mul(bis[0:1, 4:5], bis[0:1, 2:3], bis[0:1, 3:4]))
                ss(v.scalar_tensor_tensor(bis[0:1, 0:1], bis[0:1, 0:1],
                                          bis[0:1, 9:10], bis[0:1, 4:5],
                                          ALU.mult, ALU.add))
                ss(v.tensor_scalar_mul(bis[0:1, 5:6], bis[0:1, 2:3], bis[0:1, 9:10]))
                ss(v.scalar_tensor_tensor(bis[0:1, 1:2], bis[0:1, 1:2],
                                          bis[0:1, 3:4], bis[0:1, 5:6],
                                          ALU.mult, ALU.add))
            v.tensor_scalar(wrow[:], sot[:], bis[0:1, 1:2], None, ALU.is_ge)
            v.wait_ge(scg, 1)
            ss(v.tensor_mul(wrow[:], wrow[:], sgm[:]))
            v.tensor_copy(bis[0:1, 4:5], bis[0:1, 1:2]).then_inc(tvs, 1)
            for tc8 in range(NTC8):
                v.wait_ge(pvs, 2 * NITER_BIS + tc8 + 1)
                v.tensor_copy(wbc[:, tc8 * 512:(tc8 + 1) * 512],
                              pw[:, :]).then_inc(tvs, 1)
            g_idx = 0
            for j in range(NJ):
                for tc in range(NTC):
                    par = g_idx % 2
                    q_, o, _tc = group_pos(g_idx)
                    t0 = q_ * TH + tc * 512
                    v.wait_ge(mm, g_idx + 1)
                    v.wait_ge(lsx, 48 * (g_idx + 1))
                    if g_idx >= 2:
                        v.wait_ge(so, 16 * (g_idx - 1))
                    v.scalar_tensor_tensor(u1[:, par * 512:(par + 1) * 512],
                                           PS[par][:], bt[:, o:o + 1],
                                           wbc[:, t0:t0 + 512],
                                           ALU.add, ALU.mult)
                    v.tensor_add(m2[:, par * 512:(par + 1) * 512],
                                 u1[:, par * 512:(par + 1) * 512],
                                 xn1t[:, par * 512:(par + 1) * 512]).then_inc(ev, 1)
                    v.wait_ge(scg, 1 + g_idx + 1)
                    v.tensor_mul(ot[:, par * 512:(par + 1) * 512],
                                 g1[:, par * 512:(par + 1) * 512],
                                 s1t[:, par * 512:(par + 1) * 512])
                    v.tensor_add(ot[:, par * 512:(par + 1) * 512],
                                 ot[:, par * 512:(par + 1) * 512],
                                 xl1t[:, par * 512:(par + 1) * 512]).then_inc(vc, 1)
                    g_idx += 1

        @block.tensor
        def _(t):
            t.wait_ge(ld, 48)
            for it in range(NITER_BIS):
                t.wait_ge(tvs, 2 * it + 1)
                t.matmul(pmid[:, 0:1], ones1[:], bis[0:1, 2:3],
                         start=True, stop=True).then_inc(pvs, 1)
                t.wait_ge(tvs, 2 * it + 2)
                t.matmul(pcnt[0:1, 0:1], onesc[:], bis[:, 7:8],
                         start=True, stop=True).then_inc(pvs, 1)
            for tc8 in range(NTC8):
                t.wait_ge(tvs, 2 * NITER_BIS + tc8 + 1)
                t.matmul(pw[:, :], ones1b[:], wrow[0:1, tc8 * 512:(tc8 + 1) * 512],
                         start=True, stop=True).then_inc(pvs, 1)
            g_idx = 0
            for q in range(NQ):
                t.wait_ge(ld, 48 + 16 * (q + 1))
                for o in range(NO):
                    j = q * NO + o
                    t.wait_ge(cvw, j + 1)
                    for tc in range(NTC):
                        par = g_idx % 2
                        if g_idx >= 2:
                            t.wait_ge(ev, g_idx - 1)
                        for k in range(NK):
                            inst = t.matmul(
                                PS[par][:],
                                wb[:, (j % 2) * DHID + k * 128:(j % 2) * DHID + (k + 1) * 128],
                                _chunk3(xb, TH)[:, k, tc * 512:(tc + 1) * 512],
                                start=(k == 0), stop=(k == NK - 1))
                        inst.then_inc(mm, 1)
                        g_idx += 1

        @block.scalar
        def _(s):
            s.wait_ge(ld, 48)
            s.activation(out=sgm[:], in_=sot[:], func=AF.Sigmoid).then_inc(scg, 1)
            for j in range(2):
                s.wait_ge(wps, 16 * (j + 1))
                s.activation(out=wb[:, (j % 2) * DHID:(j % 2 + 1) * DHID],
                             in_=wf[:, (j % 2) * DHID:(j % 2 + 1) * DHID],
                             func=AF.Identity).then_inc(cvw, 1)

            def loads(g_idx):
                par = g_idx % 2
                if g_idx >= 2:
                    s.wait_ge(vc, g_idx - 1)
                q_, o, tc = group_pos(g_idx)
                t0 = q_ * TH + tc * 512
                s.dma_start(out=xn1t[:, par * 512:(par + 1) * 512],
                            in_=xn1T[o * 128:(o + 1) * 128, t0:t0 + 512]).then_inc(lsx, 16)
                s.dma_start(out=s1t[:, par * 512:(par + 1) * 512],
                            in_=s1T[o * 128:(o + 1) * 128, t0:t0 + 512]).then_inc(lsx, 16)
                s.dma_start(out=xl1t[:, par * 512:(par + 1) * 512],
                            in_=xl1T[o * 128:(o + 1) * 128, t0:t0 + 512]).then_inc(lsx, 16)
            loads(0)
            loads(1)
            for g_idx in range(NG):
                par = g_idx % 2
                s.wait_ge(ev, g_idx + 1)
                q_, o, tc = group_pos(g_idx)
                j = q_ * NO + o
                s.activation(out=g1[:, par * 512:(par + 1) * 512],
                             in_=m2[:, par * 512:(par + 1) * 512],
                             func=AF.Silu).then_inc(scg, 1)
                if g_idx + 2 < NG:
                    loads(g_idx + 2)
                if tc == NTC - 1 and j + 2 < NJ:
                    s.wait_ge(wps, 16 * (j + 3))
                    s.wait_ge(mm, NTC * (j + 1))
                    s.activation(out=wb[:, (j % 2) * DHID:(j % 2 + 1) * DHID],
                                 in_=wf[:, (j % 2) * DHID:(j % 2 + 1) * DHID],
                                 func=AF.Identity).then_inc(cvw, 1)
    return nc

CORES = list(range(NCORES))


def _cores_bs():
    return [(c // 2, slice((c % 2) * TLOC, (c % 2 + 1) * TLOC)) for c in CORES]


def _run(name, builder, ins):
    nc = _cached(name, builder)
    return run_bass_kernel_spmd(nc, ins, core_ids=CORES)


def kernel(L0_attn_x, L0_attn_weight_q, L0_attn_weight_k, L0_attn_weight_v,
           L0_attn_theta, L0_ffn_w1_w, L0_ffn_w1_s, L0_ffn_w2_w, L0_ffn_w2_s,
           L0_ffn_h1_w, L0_ffn_h1_s, L0_ffn_h2_w, L0_ffn_h2_s, L0_ffn_w3_w,
           L0_ffn_w3_s, L1_attn_W_in, L1_attn_b_in, L1_attn_conv_w,
           L1_attn_conv_b, L1_attn_W_x, L1_attn_b_x, L1_attn_A_log,
           L1_attn_D_param, L1_attn_W_out, L1_attn_b_out, L1_ffn_w_router,
           L1_ffn_W1, L1_ffn_b1, L1_ffn_W2, L1_ffn_b2):
    x = np.asarray(L0_attn_x, dtype=np.float32)
    cb = _cores_bs()
    A = np.ascontiguousarray

    res = _run("ln", build_ln, [{"x": A(x[b, sl])} for b, sl in cb])
    xn = [res.results[c]["xn"] for c in CORES]

    ins = []
    for c, (b, sl) in enumerate(cb):
        if c % 2 == 0:
            halo = np.zeros((HHALO, D), np.float32)
        else:
            halo = xn[c - 1][TLOC - HHALO:TLOC]
        ins.append({"xnT": A(np.concatenate([halo, xn[c]], axis=0).T),
                    "wq": np.asarray(L0_attn_weight_q, np.float32),
                    "wk": np.asarray(L0_attn_weight_k, np.float32),
                    "wv": np.asarray(L0_attn_weight_v, np.float32)})
    res = _run("ret", build_retention, ins)
    sT = [res.results[c]["sT"] for c in CORES]

    w1t = A(np.asarray(L0_ffn_w1_w, np.float32).T)
    w2t = A(np.asarray(L0_ffn_w2_w, np.float32).T)
    s1r = A(np.asarray(L0_ffn_w1_s, np.float32).reshape(32, 128).T)
    s2r = A(np.asarray(L0_ffn_w2_s, np.float32).reshape(32, 128).T)
    xnT = [A(xn[c].T) for c in CORES]
    res = _run("ffa", build_ffn_a,
               [{"xnT": xnT[c], "w1t": w1t, "w2t": w2t, "s1": s1r, "s2": s2r}
                for c in CORES])
    h = [res.results[c]["h1"] for c in CORES]

    for wkey, skey in ((L0_ffn_h1_w, L0_ffn_h1_s), (L0_ffn_h2_w, L0_ffn_h2_s)):
        wt = A(np.asarray(wkey, np.float32).T)
        scr = A(np.asarray(skey, np.float32).reshape(32, 128).T)
        res = _run("ffm", build_ffn_mid,
                   [{"hin": h[c], "wt": wt, "sc": scr} for c in CORES])
        h = [res.results[c]["hout"] for c in CORES]

    w3t = A(np.asarray(L0_ffn_w3_w, np.float32).T)
    s3r = A(np.asarray(L0_ffn_w3_s, np.float32).reshape(8, 128).T)
    res = _run("fff", build_ffn_final,
               [{"hin": h[c], "w3t": w3t, "s3": s3r, "sT": sT[c],
                 "xT": A(x[b, sl].T)} for c, (b, sl) in enumerate(cb)])
    xl1T = [res.results[c]["outT"] for c in CORES]

    res = _run("ln", build_ln, [{"x": A(xl1T[c].T)} for c in CORES])
    xn1 = [res.results[c]["xn"] for c in CORES]

    b_in_re = A(np.asarray(L1_attn_b_in, np.float32).reshape(16, 128).T)
    convw_re = A(np.asarray(L1_attn_conv_w, np.float32)
                 .reshape(8, 128, 4).transpose(1, 0, 2).reshape(128, 32))
    convb_re = A(np.asarray(L1_attn_conv_b, np.float32).reshape(8, 128).T)
    bx_re = A(np.asarray(L1_attn_b_x, np.float32).reshape(32, 1))
    ins = []
    for c, (b, sl) in enumerate(cb):
        if c % 2 == 0:
            halo = np.zeros((MHALO, D), np.float32)
        else:
            halo = xn1[c - 1][TLOC - MHALO:TLOC]
        ins.append({"xn1T": A(np.concatenate([halo, xn1[c]], axis=0).T),
                    "w_in": np.asarray(L1_attn_W_in, np.float32),
                    "b_in": b_in_re, "convw": convw_re, "convb": convb_re,
                    "wx": np.asarray(L1_attn_W_x, np.float32), "bx": bx_re})
    res = _run("mpre", build_mamba_pre, ins)
    xcT = [res.results[c]["xcT"] for c in CORES]
    bcv = [res.results[c]["bc"] for c in CORES]
    sgT = [res.results[c]["sgT"] for c in CORES]

    A_log = np.asarray(L1_attn_A_log, np.float32)
    D_par = np.asarray(L1_attn_D_param, np.float32)
    SEL = np.zeros((128, 4 * 128), np.float32)
    for dg in range(4):
        for p in range(128):
            SEL[p, dg * 128 + dg * 32 + p // 4] = 1.0
    ins = []
    for c in CORES:
        b, dh = c // 2, c % 2
        dsl = slice(dh * DHALF, (dh + 1) * DHALF)
        xcfull = np.concatenate([xcT[2 * b], xcT[2 * b + 1]], axis=1)
        bcfull = np.concatenate([bcv[2 * b], bcv[2 * b + 1]], axis=1)
        sgfull = np.concatenate([sgT[2 * b], sgT[2 * b + 1]], axis=1)
        al = A_log[dsl]
        alog_re = A(al.reshape(4, 4, 32, 4, 4).transpose(2, 4, 0, 1, 3)
                    .reshape(128, 64))
        dp_re = A(D_par[dsl].reshape(4, 128).T)
        ins.append({"xch": A(xcfull[dsl]), "bcm": A(bcfull.astype(np.float32)),
                    "alog": alog_re, "dpar": dp_re,
                    "sgh": A(sgfull[dsl].astype(np.float32)), "sel": SEL})
    res = _run("scan", build_scan, ins)
    yhs = [res.results[c]["yh"] for c in CORES]

    b_out_re = A(np.asarray(L1_attn_b_out, np.float32).reshape(8, 128).T)
    ins = []
    for c, (b, sl) in enumerate(cb):
        h2_ = c % 2
        yT_c = np.concatenate(
            [yhs[2 * b][:, h2_ * TLOC:(h2_ + 1) * TLOC],
             yhs[2 * b + 1][:, h2_ * TLOC:(h2_ + 1) * TLOC]], axis=0)
        ins.append({"yT": A(yT_c), "w_out": np.asarray(L1_attn_W_out, np.float32),
                    "b_out": b_out_re})
    res = _run("wout", build_wout, ins)
    s1T = [res.results[c]["s1T"] for c in CORES]

    xn1T = [A(xn1[c].T) for c in CORES]
    b1_re = A(np.asarray(L1_ffn_b1, np.float32).reshape(32, 128).T)
    wrt_re = A(np.asarray(L1_ffn_w_router, np.float32).reshape(8, 128).T)
    res = _run("moda", build_mod_a,
               [{"xn1T": xn1T[c], "w1": np.asarray(L1_ffn_W1, np.float32),
                 "b1": b1_re, "wrt": wrt_re} for c in CORES])
    hmods = [res.results[c]["hmod"] for c in CORES]
    scs = [res.results[c]["sc"] for c in CORES]

    b2_re = A(np.asarray(L1_ffn_b2, np.float32).reshape(8, 128).T)
    ins = []
    for c, (b, sl) in enumerate(cb):
        sc_b = np.concatenate([scs[2 * b][0], scs[2 * b + 1][0]])
        ins.append({"sc_all": A(sc_b.reshape(64, 128).T),
                    "sc_own": A(scs[c]),
                    "hmod": hmods[c], "w2": np.asarray(L1_ffn_W2, np.float32),
                    "b2": b2_re, "xn1T": xn1T[c], "xl1T": xl1T[c],
                    "s1T": s1T[c]})
    res = _run("modb", build_mod_b, ins)

    out = np.zeros((B, L, D), np.float32)
    for c, (b, sl) in enumerate(cb):
        out[b, sl, :] = res.results[c]["outT"].T
    return out
